# version 03 (final)
# baseline (speedup 1.0000x reference)
import math
import numpy as np
import ml_dtypes
import concourse.bass as bass
import concourse.mybir as mybir
from concourse.bass_utils import run_bass_kernel_spmd
from contextlib import ExitStack

F32 = mybir.dt.float32
BF16 = mybir.dt.bfloat16
U8 = mybir.dt.uint8
AF = mybir.ActivationFunctionType
ALU = mybir.AluOpType
AX = mybir.AxisListType
BF = ml_dtypes.bfloat16

B, T, D = 4, 4096, 2048
EPS_RMS = float(np.finfo(np.float32).eps)
EPS_LN = 1e-5
LN_SCALE = -0.5 * math.log(128.0)
NEG = -1e30
KB = 1024
ENG = ("sync", "gpsimd", "tensor", "vector", "scalar")


class Sem:
    __slots__ = ("name", "n", "h")

    def __init__(self, name):
        self.name, self.n, self.h = name, 0, None


class Prog:
    def __init__(self):
        self.q = {e: [] for e in ENG}
        self.lastw = {e: {} for e in ENG}

    def w(self, eng, sem, val=None):
        if val is None:
            val = sem.n
        if val <= 0:
            return
        if self.lastw[eng].get(sem.name, -1) >= val:
            return
        self.lastw[eng][sem.name] = val
        self.q[eng].append(("w", sem, val))

    def o(self, eng, fn, sem=None, inc=1):
        self.q[eng].append(("o", fn, sem, inc))
        if sem is not None:
            sem.n += inc
            return sem.n
        return None

    def emit(self, block):
        for name in ENG:
            items = self.q[name]
            dec = getattr(block, name)

            def mk(items):
                def body(eng):
                    for it in items:
                        if it[0] == "w":
                            eng.wait_ge(it[1].h, it[2])
                        else:
                            _, fn, sem, inc = it
                            ins = fn(eng)
                            if sem is not None:
                                ins.then_inc(sem.h, inc)
                return body
            dec(mk(items))


class Overlay:
    def __init__(self, nc, total):
        self.nc, self.total, self.items = nc, total, []

    def at(self, off, name, shape, dtype):
        per = 1
        for s in shape[1:]:
            per *= s
        per *= mybir.dt.size(dtype)
        assert off % 32 == 0, (name, off)
        assert off + per <= self.total, (name, off, per)
        h = self.nc.virtual_sbuf_tensor(name, shape, dtype)
        self.items.append((h, off))
        return h.ap()

    def finalize(self, ctx):
        arena = ctx.enter_context(
            self.nc.sbuf_tensor("arena", [128, self.total], U8))
        base = self.nc.lookup_mloc(arena).addr
        for h, off in self.items:
            self.nc._tensor(h.name, h.shape, h.dtype, type="SB")
            m = self.nc.lookup_mloc(h)
            m.allocated = True
            m.addr = base + off



def _prep_shared(inp):
    f = np.ascontiguousarray
    o = {}
    o["wq"] = f(inp["L0_attn_Wq"].reshape(2, 8, 128, 4, 512)
                .transpose(3, 0, 2, 1, 4)).reshape(4, 2, 128, 4096)
    o["wk"] = f(inp["L0_attn_Wk"].reshape(16, 128, 512)
                .transpose(1, 0, 2)).reshape(128, 8192)
    o["wv"] = f(inp["L0_attn_Wv"].reshape(16, 128, 512)
                .transpose(1, 0, 2)).reshape(128, 8192)
    o["wproj"] = f(inp["L0_attn_Wproj"].reshape(2, 8, 128, 4, 512)
                   .transpose(3, 0, 2, 1, 4)).reshape(4, 2, 128, 4096)
    for nm, key in (("qa", "L0_attn_q_A"), ("ka", "L0_attn_k_A"),
                    ("va", "L0_attn_v_A"), ("pa", "L0_attn_p_A")):
        o[nm] = f(inp[key].reshape(16, 128, 16).transpose(1, 0, 2)).reshape(128, 256)
    o["qb"] = f(inp["L0_attn_q_B"])
    o["kb"] = f(inp["L0_attn_k_B"])
    o["vb"] = f(inp["L0_attn_v_B"])
    o["pb"] = f(inp["L0_attn_p_B"])
    o["gain"] = f(inp["L0_attn_q_gain"].reshape(1, 16).astype(np.float32))
    o["gwt"] = f(inp["L0_ffn_gate_weight"].T.reshape(16, 128, 16)
                 .transpose(1, 0, 2)).reshape(128, 256)
    o["ebias"] = f(inp["L0_ffn_e_score_correction_bias"].reshape(1, 16)
                   .astype(np.float32))
    o["gate_t"] = f(inp["L0_ffn_gate_w"].reshape(16, 8, 128, 16, 128)
                    .transpose(0, 1, 4, 3, 2)).reshape(16, 8, 128, 2048)
    o["up_t"] = f(inp["L0_ffn_up_w"].reshape(16, 8, 128, 16, 128)
                  .transpose(0, 1, 4, 3, 2)).reshape(16, 8, 128, 2048)
    o["down_t"] = f(inp["L0_ffn_down_w"].reshape(16, 4, 512, 8, 128)
                    .transpose(0, 1, 4, 3, 2)).reshape(16, 4, 128, 4096)
    o["shg_t"] = f(inp["L0_ffn_sh_gate_w"].reshape(16, 128, 16, 128)
                   .transpose(0, 3, 2, 1)).reshape(16, 128, 2048)
    o["shu_t"] = f(inp["L0_ffn_sh_up_w"].reshape(16, 128, 16, 128)
                   .transpose(0, 3, 2, 1)).reshape(16, 128, 2048)
    sd = inp["L0_ffn_sh_down_w"].reshape(4, 512, 16, 128).transpose(2, 0, 3, 1)
    o["shd_t"] = f(sd.reshape(2, 8, 4, 128, 512).transpose(0, 2, 3, 1, 4)
                   ).reshape(2, 4, 128, 4096)
    inv = (1.0 / 10000.0 ** (np.arange(0, 64, 2, dtype=np.float32) / 64.0)
           ).astype(np.float32)
    fr = np.arange(T, dtype=np.float32)[:, None] * inv[None, :]
    cos = np.cos(fr).astype(np.float32)
    sin = np.sin(fr).astype(np.float32)
    o["cosk"] = f(cos.reshape(32, 128, 32).transpose(1, 0, 2)).reshape(128, 1024)
    o["sink"] = f(sin.reshape(32, 128, 32).transpose(1, 0, 2)).reshape(128, 1024)
    o["_cos"], o["_sin"] = cos, sin
    o["idbf"] = np.eye(128, dtype=BF)
    o["idf32"] = np.eye(128, dtype=np.float32)
    return o


def _prep_core(inp, sh, core):
    f = np.ascontiguousarray
    b, hc = core // 2, core % 2
    x = np.asarray(inp["L0_attn_x"], dtype=np.float32)
    o = {}
    o["xkv"] = f(x[b])
    o["xq"] = f(x[b].reshape(32, 128, D)[hc::2].reshape(2048, D))
    rows = (np.arange(2048) // 128 * 2 + hc) * 128 + np.arange(2048) % 128
    o["cosq"] = f(sh["_cos"][rows].reshape(16, 128, 32).transpose(1, 0, 2)
                  ).reshape(128, 512)
    o["sinq"] = f(sh["_sin"][rows].reshape(16, 128, 32).transpose(1, 0, 2)
                  ).reshape(128, 512)
    r = np.arange(128)[:, None]
    kk = np.arange(512)[None, :]
    mqs = np.empty((2, 128, 512), np.float32)
    for par in range(2):
        pidx = hc if par == 0 else 2 + hc
        mqs[par] = np.where(kk <= 128 * pidx + r, 0.0, NEG)
    o["mqs"] = f(mqs.transpose(1, 0, 2).astype(BF)).reshape(128, 1024)
    t_ = np.arange(256)[None, :]
    msq = np.empty((4, 128, 256), np.float32)
    for r4 in range(4):
        condA = (128 * r4 + r) <= (128 * hc + t_)
        condB = (128 * r4 + r) <= (128 * (hc + 2) + (t_ - 128))
        msq[r4] = np.where(np.where(t_ < 128, condA, condB), 0.0, NEG)
    o["msq"] = f(msq.transpose(1, 0, 2).astype(BF)).reshape(128, 1024)
    return o


PHASES = ("a1", "a2", "b", "c", "moe")


def build_program(phases=PHASES, dbg=(), b_probe=None, moe_stop=None):
    nc = bass.Bass("TRN2", target_bir_lowering=False, debug=False)

    def din(name, shape, dtype=F32):
        return nc.dram_tensor(name, shape, dtype, kind="ExternalInput").ap()

    wq_i = din("wq", [4, 2, 128, 4096])
    wk_i = din("wk", [128, 8192])
    wv_i = din("wv", [128, 8192])
    wp_i = din("wproj", [4, 2, 128, 4096])
    qa_i = din("qa", [128, 256]); qb_i = din("qb", [16, 2048])
    ka_i = din("ka", [128, 256]); kb_i = din("kb", [16, 512])
    va_i = din("va", [128, 256]); vb_i = din("vb", [16, 512])
    pa_i = din("pa", [128, 256]); pb_i = din("pb", [16, 2048])
    gain_i = din("gain", [1, 16])
    gwt_i = din("gwt", [128, 256])
    ebias_i = din("ebias", [1, 16])
    gate_i = din("gate_t", [16, 8, 128, 2048])
    up_i = din("up_t", [16, 8, 128, 2048])
    down_i = din("down_t", [16, 4, 128, 4096])
    shg_i = din("shg_t", [16, 128, 2048])
    shu_i = din("shu_t", [16, 128, 2048])
    shd_i = din("shd_t", [2, 4, 128, 4096])
    cosk_i = din("cosk", [128, 1024]); sink_i = din("sink", [128, 1024])
    idbf_i = din("idbf", [128, 128], BF16)
    idf32_i = din("idf32", [128, 128])
    xq_i = din("xq", [2048, 2048])
    xkv_i = din("xkv", [4096, 2048])
    cosq_i = din("cosq", [128, 512]); sinq_i = din("sinq", [128, 512])
    mqs_i = din("mqs", [128, 1024], BF16)
    msq_i = din("msq", [128, 1024], BF16)

    out_d = nc.dram_tensor("out", [2048, 2048], F32, kind="ExternalOutput").ap()
    zsp = nc.dram_tensor("zsp", [16, 128, 2048], F32, kind="Internal").ap()
    x1sp = nc.dram_tensor("x1sp", [8, 128, 2048], BF16, kind="Internal").ap()

    dbg_outs = {}

    def ddbg(name, shape, dtype):
        ap = nc.dram_tensor("dbg_" + name, shape, dtype, kind="ExternalOutput").ap()
        dbg_outs[name] = ap
        return ap

    TOT = 212448
    al = Overlay(nc, TOT)
    q_tm = al.at(0, "q_tm", [128, 32768], BF16)
    routed = al.at(0, "routed", [128, 16384], F32)
    x_fmh = al.at(64 * KB, "x_fmh", [128, 16384], BF16)
    wv_a1 = al.at(96 * KB, "wv_a1", [128, 8192], BF16)
    wk_sb = al.at(64 * KB, "wk_sb", [128, 8192], BF16)
    v_tm = al.at(80 * KB, "v_tm", [128, 8192], BF16)
    v_tm2 = al.at(112 * KB, "v_tm2", [128, 8192], BF16)
    s_row = al.at(64 * KB, "s_row", [128, 4096], F32)
    mqs_s = al.at(96 * KB, "mqs_s", [128, 1024], BF16)
    msq_s = al.at(98 * KB, "msq_s", [128, 1024], BF16)
    qfm_t = al.at(100 * KB, "qfm_c", [128, 512], BF16)
    qfm_c = [qfm_t[:, 0:256], qfm_t[:, 256:512]]
    p_ring_t = al.at(101 * KB, "p_ring", [128, 1024], BF16)
    p_ring = [p_ring_t[:, i * 256:(i + 1) * 256] for i in range(4)]
    junk = al.at(103 * KB, "junk", [128, 512], BF16)
    x1_fm = al.at(64 * KB, "x1_fm", [128, 16384], BF16)
    wp_ring_t = al.at(96 * KB, "wp_ring", [128, 12288], BF16)
    wp_ring = [wp_ring_t[:, i * 4096:(i + 1) * 4096] for i in range(3)]
    x1bf_t = al.at(120 * KB, "x1bf", [128, 4096], BF16)
    x1bf = [x1bf_t[:, 0:2048], x1bf_t[:, 2048:4096]]
    dw_ring_t = al.at(96 * KB, "dw_ring", [128, 8192], BF16)
    dw_ring = [dw_ring_t[:, 0:4096], dw_ring_t[:, 4096:8192]]
    silu_t = al.at(112 * KB, "silu_s", [128, 1024], F32)
    silu_s = [silu_t[:, 0:512], silu_t[:, 512:1024]]
    wq_ring_t = al.at(128 * KB, "wq_ring", [128, 12288], BF16)
    wq_ring = [wq_ring_t[:, i * 4096:(i + 1) * 4096] for i in range(3)]
    k_fm = al.at(128 * KB, "k_fm", [128, 16384], BF16)
    v_ofm = al.at(160 * KB, "v_ofm", [128, 8192], BF16)
    act_sb = al.at(128 * KB, "act_sb", [128, 8192], BF16)
    g_ring_t = al.at(144 * KB, "g_ring", [128, 6144], BF16)
    g_ring = [g_ring_t[:, i * 2048:(i + 1) * 2048] for i in range(3)]
    u_ring_t = al.at(156 * KB, "u_ring", [128, 6144], BF16)
    u_ring = [u_ring_t[:, i * 2048:(i + 1) * 2048] for i in range(3)]
    x1ld_t = al.at(168 * KB, "x1ld", [128, 4096], BF16)
    x1ld = [x1ld_t[:, 0:2048], x1ld_t[:, 2048:4096]]
    ln2scr = al.at(168 * KB, "ln2scr", [128, 2048], F32)
    RD = 176 * KB
    x_stage_t = al.at(RD, "x_stage", [128, 4096], BF16)
    x_stage = [x_stage_t[:, 0:2048], x_stage_t[:, 2048:4096]]
    cosq = al.at(RD + 8 * KB, "cosq", [128, 512], F32)
    sinq = al.at(RD + 10 * KB, "sinq", [128, 512], F32)
    qb_sb = al.at(RD + 12 * KB, "qb_sb", [16, 2048], BF16)
    qa_sb = al.at(RD + 16 * KB, "qa_sb", [128, 256], BF16)
    va1_sb = al.at(RD + 16 * KB + 512, "va1_sb", [128, 256], BF16)
    gain_bc = al.at(RD + 17 * KB, "gain_bc", [128, 16], F32)
    vb_a1 = al.at(RD + 17 * KB + 64, "vb_a1", [16, 512], BF16)
    xlrT = al.at(RD + 18 * KB + 128, "xlrT", [16, 2048], BF16)
    lr_bf = al.at(RD + 22 * KB + 128, "lr_bf", [128, 16], BF16)
    lrT_scr = al.at(RD + 22 * KB + 192, "lrT_scr", [16, 128], BF16)
    q_f32 = al.at(RD + 23 * KB, "q_f32", [128, 512], F32)
    q_sq = al.at(RD + 25 * KB, "q_sq", [128, 512], F32)
    q_st = al.at(RD + 27 * KB, "q_st", [128, 16], F32)
    v_bf_a1 = al.at(RD + 27 * KB + 64, "v_bf_a1", [128, 512], BF16)
    cosk = al.at(RD + 8 * KB, "cosk", [128, 1024], F32)
    sink = al.at(RD + 12 * KB, "sink", [128, 1024], F32)
    ka_sb = al.at(RD + 16 * KB, "ka_sb", [128, 256], BF16)
    va_sb = al.at(RD + 16 * KB + 512, "va_sb", [128, 256], BF16)
    kb_sb = al.at(RD + 17 * KB, "kb_sb", [16, 512], BF16)
    vb_sb = al.at(RD + 18 * KB, "vb_sb", [16, 512], BF16)
    kvscr = al.at(RD + 19 * KB, "kvscr", [128, 512], F32)
    kvsq = al.at(RD + 21 * KB, "kvsq", [128, 512], F32)
    kv_bf = al.at(RD + 23 * KB, "kv_bf", [128, 512], BF16)
    kvst = al.at(RD + 24 * KB, "kvst", [128, 16], F32)
    lr_bf2 = al.at(RD + 24 * KB + 64, "lr_bf2", [128, 16], BF16)
    lrT_scr2 = al.at(RD + 24 * KB + 128, "lrT_scr2", [16, 128], BF16)
    x_fmkv = al.at(RD + 24 * KB + 512, "x_fmkv", [128, 2048], BF16)
    rvrow = al.at(RD, "rvrow", [1, 256], F32)
    rv2row_d = al.at(RD + 1 * KB, "rv2row_d", [1, 512], F32)
    rsrow_d = al.at(RD + 3 * KB, "rsrow_d", [1, 512], F32)
    negmrow_d = al.at(RD + 5 * KB, "negmrow_d", [1, 512], BF16)
    dotrow = al.at(RD + 12 * KB, "dotrow", [1, 256], F32)
    dot2row = al.at(RD + 13 * KB, "dot2row", [1, 256], F32)
    mtile = al.at(RD + 6 * KB, "mtile", [128, 8], F32)
    ssum = al.at(RD + 6 * KB + 32, "ssum", [128, 8], F32)
    negm32 = al.at(RD + 6 * KB + 64, "negm32", [128, 2], F32)
    negmbf = al.at(RD + 6 * KB + 96, "negmbf", [128, 2], BF16)
    rsblk = al.at(RD + 6 * KB + 128, "rsblk", [128, 2], F32)
    mred = al.at(RD + 6 * KB + 160, "mred", [128, 1], F32)
    y_ev = al.at(RD + 8 * KB, "y_ev", [128, 256], F32)
    t1_b = al.at(RD + 9 * KB, "t1_b", [128, 256], BF16)
    y1_b = al.at(RD + 10 * KB, "y1_b", [128, 256], F32)
    proj_b = al.at(RD + 11 * KB, "proj_b", [128, 256], F32)
    xsl_t = al.at(RD, "xsl", [128, 1024], F32)
    xsl = [xsl_t[:, 0:512], xsl_t[:, 512:1024]]
    zsl_t = al.at(RD + 4 * KB, "zsl", [128, 1024], F32)
    zsl = [zsl_t[:, 0:512], zsl_t[:, 512:1024]]
    zrel = al.at(RD + 8 * KB, "zrel", [128, 2048], F32)
    xpaT = al.at(RD + 16 * KB, "xpaT", [16, 2048], BF16)
    pb_sb = al.at(RD + 20 * KB, "pb_sb", [16, 2048], BF16)
    pa_sb = al.at(RD + 24 * KB, "pa_sb", [128, 256], BF16)
    lr_bf3 = al.at(RD + 24 * KB + 512, "lr_bf3", [128, 16], BF16)
    lrT_scr3 = al.at(RD + 24 * KB + 576, "lrT_scr3", [16, 128], BF16)
    zslq = al.at(RD + 25 * KB, "zslq", [128, 512], F32)
    zrel2_t = al.at(RD, "zrel2", [128, 4096], F32)
    zrel2 = [zrel2_t[:, 0:2048], zrel2_t[:, 2048:4096]]
    w_all = al.at(RD + 16 * KB, "w_all", [128, 128], F32)
    s_all = al.at(RD + 16 * KB + 512, "s_all", [128, 128], F32)
    c_all = al.at(RD + 17 * KB, "c_all", [128, 128], F32)
    cm_all = al.at(RD + 17 * KB + 512, "cm_all", [128, 128], F32)
    r_scr = al.at(RD + 18 * KB, "r_scr", [128, 128], F32)
    r_scr2 = al.at(RD + 18 * KB + 512, "r_scr2", [128, 128], F32)
    gsc = al.at(RD + 19 * KB, "gsc", [128, 32], F32)
    gsc2 = al.at(RD + 19 * KB + 128, "gsc2", [128, 48], F32)
    thg = al.at(RD + 19 * KB + 320, "thg", [128, 8], F32)
    gmask = al.at(RD + 19 * KB + 352, "gmask", [128, 32], F32)
    wsum = al.at(RD + 19 * KB + 480, "wsum", [128, 8], F32)
    mext = al.at(RD + 19 * KB + 512, "mext", [128, 64], F32)
    ebias_bc = al.at(RD + 20 * KB, "ebias_bc", [128, 16], F32)
    bnbuf = al.at(RD + 20 * KB + 64, "bnbuf", [128, 24], F32)
    bnagg = al.at(RD + 20 * KB + 160, "bnagg", [128, 2], F32)
    ln2st = al.at(RD + 20 * KB + 224, "ln2st", [128, 4], F32)
    RP = 204 * KB + 768
    idbf = al.at(RP, "idbf", [128, 128], BF16)
    idf32 = al.at(RP + 256, "idf32", [128, 128], F32)
    ones_bfr = al.at(RP + 768, "ones_bfr", [1, 128], BF16)
    ones_f32r = al.at(RP + 1024, "ones_f32r", [1, 128], F32)
    ones_bfc = al.at(RP + 1536, "ones_bfc", [128, 1], BF16)
    ln_sum = al.at(RP + 1568, "ln_sum", [128, 16], F32)
    ln_sq = al.at(RP + 1632, "ln_sq", [128, 16], F32)
    ln_mu = al.at(RP + 1696, "ln_mu", [128, 16], F32)
    ln_r = al.at(RP + 1760, "ln_r", [128, 16], F32)
    rv_tm = al.at(RP + 1824, "rv_tm", [128, 64], F32)
    tmp1 = al.at(RP + 2080, "tmp1", [128, 1], F32)
    tmp2 = al.at(RD + 27 * KB + 128, "tmp2", [128, 1], F32)
    gwt_sb = al.at(RP + 2112, "gwt_sb", [128, 256], BF16)
    gain_sb = al.at(RP + 2624, "gain_sb", [1, 16], F32)
    ebias_sb = al.at(RP + 2688, "ebias_sb", [1, 16], F32)
    cns = al.at(RP + 2752, "cns", [128, 4], F32)

    def vtm(tt):
        if tt < 16:
            return v_tm[:, tt * 512:(tt + 1) * 512]
        return v_tm2[:, (tt - 16) * 512:(tt - 16 + 1) * 512]

    p = Prog()
    sW = Sem("sW"); sL = Sem("sL"); sS = Sem("sS")
    sPE = Sem("sPE"); sDV = Sem("sDV"); sAC = Sem("sAC")
    sems = [sW, sL, sS, sPE, sDV, sAC]

    ctx = ExitStack()
    pbk = [ctx.enter_context(nc.psum_tensor(f"pb{i}", [128, 512], F32))
           for i in range(6)]
    pb = [x[:] for x in pbk]
    pbt_h = ctx.enter_context(nc.psum_tensor("pbt", [128, 1024], BF16))
    pbt = pbt_h[:]
    pbs_h = ctx.enter_context(nc.psum_tensor("pbs", [128, 512], F32))
    pbs = pbs_h[:]
    al.finalize(ctx)
    for s in sems:
        s.h = ctx.enter_context(nc.semaphore(s.name))

    def dma_g(dst, src):
        return p.o("gpsimd", lambda e, d=dst, s=src: e.dma_start(out=d, in_=s), sW, 16)

    def dma_s(dst, src):
        return p.o("sync", lambda e, d=dst, s=src: e.dma_start(out=d, in_=s), sL, 16)

    def dma_st(dst, src):
        return p.o("sync", lambda e, d=dst, s=src: e.dma_start(out=d, in_=s), sS, 16)

    def mm(out, lhsT, rhs, start, stop, tok=False):
        return p.o("tensor",
                   lambda e, o=out, l=lhsT, r=rhs, st=start, sp=stop:
                   e.matmul(o, l, r, start=st, stop=sp, skip_group_check=True),
                   sPE if tok else None, 1)

    def transp(out, in_, ident=None, tok=False):
        idm = idbf if ident is None else ident
        return p.o("tensor", lambda e, o=out, i=in_, d=idm: e.transpose(o, i, d),
                   sPE if tok else None, 1)

    def vo(fn, tok=True):
        return p.o("vector", fn, sDV if tok else None, 1)

    def so(fn, tok=True):
        return p.o("scalar", fn, sAC if tok else None, 1)

    def vf():
        p.w("vector", sDV)

    def sf():
        p.w("scalar", sAC)

    def join():
        for e in ENG:
            for s in sems:
                p.w(e, s)

    def transpose_tile(src_bf, dst_fn, pre_pe=None, n_chunks=16):
        for grp in range((n_chunks + 7) // 8):
            lo, hi = grp * 8, min(grp * 8 + 8, n_chunks)
            p.w("tensor", sDV)
            if pre_pe is not None:
                pre_pe()
            for j, c in enumerate(range(lo, hi)):
                transp(pbt[:, j * 128:(j + 1) * 128],
                       src_bf[:, c * 128:(c + 1) * 128], tok=(c == hi - 1))
            p.w("vector", sPE)
            vo(lambda e, g=grp, n=hi - lo: e.tensor_copy(
                dst_fn(g, n),
                pbt.rearrange("p (a b) -> p a b", b=128)[:, 0:n, :]), tok=True)

    def lowrank_T(x_chunk_fn, a_ap, dst_T, scr_bf, pre_pe=None):
        p.w("tensor", sDV)
        if pre_pe is not None:
            pre_pe()
        for hc in range(16):
            mm(pbs[:, 0:16], x_chunk_fn(hc), a_ap[:, hc * 16:(hc + 1) * 16],
               hc == 0, hc == 15, tok=(hc == 15))
        p.w("vector", sPE)
        vo(lambda e, s=scr_bf: e.tensor_copy(s, pbs[:, 0:16]), tok=True)
        p.w("tensor", sDV)
        transp(pbt[0:16, 0:128], scr_bf, tok=True)
        p.w("vector", sPE)
        vo(lambda e, d=dst_T: e.tensor_copy(d, pbt[0:16, 0:128]), tok=True)

    def rope_block(dst_bf, src_f32, cos_ap, sin_ap, scr):
        for hh in range(4):
            b0 = hh * 128
            x1 = src_f32[:, b0:b0 + 32]
            x2 = src_f32[:, b0 + 32:b0 + 64]
            s1a = scr[:, b0:b0 + 32]
            s2a = scr[:, b0 + 32:b0 + 64]
            s1b = scr[:, b0 + 64:b0 + 96]
            s2b = scr[:, b0 + 96:b0 + 128]
            vo(lambda e, a=x1, c=cos_ap, o=s1a: e.tensor_tensor(o, a, c, ALU.mult))
            vo(lambda e, a=x2, c=sin_ap, o=s2a: e.tensor_tensor(o, a, c, ALU.mult))
            vo(lambda e, a=x1, c=sin_ap, o=s1b: e.tensor_tensor(o, a, c, ALU.mult))
            vo(lambda e, a=x2, c=cos_ap, o=s2b: e.tensor_tensor(o, a, c, ALU.mult))
            vf()
            vo(lambda e, a=s1a, b2=s2a, o=dst_bf[:, b0:b0 + 32]:
               e.tensor_tensor(o, a, b2, ALU.add))
            vo(lambda e, a=s2b, b2=s1b, o=dst_bf[:, b0 + 32:b0 + 64]:
               e.tensor_tensor(o, a, b2, ALU.subtract))

    dma_s(idbf[:], idbf_i)
    dma_s(idf32[:], idf32_i)
    dma_s(gain_sb[:], gain_i)
    dma_s(cosq[:], cosq_i)
    dma_s(sinq[:], sinq_i)
    dma_g(qa_sb[:], qa_i)
    dma_g(va1_sb[:], va_i)
    dma_g(qb_sb[:], qb_i)
    dma_g(vb_a1[:], vb_i)
    dma_g(wv_a1[:], wv_i)
    dma_g(gwt_sb[:], gwt_i)
    dma_s(ebias_sb[:], ebias_i)
    a0_w, a0_l = sW.n, sL.n
    vo(lambda e: e.memset(ones_bfr[:], 1.0))
    vo(lambda e: e.memset(ones_f32r[:], 1.0))
    vo(lambda e: e.memset(ones_bfc[:], 1.0))
    vo(lambda e: e.memset(cns[:, 0:1], EPS_RMS))
    vo(lambda e: e.memset(cns[:, 1:2], LN_SCALE))
    vo(lambda e: e.memset(cns[:, 2:3], EPS_LN), tok=True)
    p.w("tensor", sL, a0_l)
    p.w("tensor", sDV)
    mm(pbs[:, 0:16], ones_f32r[:], gain_sb[:], True, True, tok=True)
    p.w("vector", sPE)
    vo(lambda e: e.tensor_copy(gain_bc[:], pbs[:, 0:16]), tok=True)

    if "a1" in phases:
        wq_free = {}
        for th in range(2):
            stage_free = {}
            for i in range(8):
                tt = th * 8 + i
                if i >= 2:
                    p.w("gpsimd", sPE, stage_free[i - 2])
                dma_g(x_stage[i % 2], xq_i[tt * 128:(tt + 1) * 128, :])
                wtok = sW.n
                transpose_tile(
                    x_stage[i % 2],
                    lambda g, n, i=i: x_fmh.rearrange(
                        "p (hc c) -> p hc c", c=1024)[:, g * 8:g * 8 + n,
                                                      i * 128:(i + 1) * 128],
                    pre_pe=lambda w=wtok: p.w("tensor", sW, w))
                stage_free[i] = sPE.n

                def xch(hc, i=i):
                    return x_fmh[:, hc * 1024 + i * 128: hc * 1024 + (i + 1) * 128]
                lowrank_T(xch, qa_sb, xlrT[0:16, i * 128:(i + 1) * 128],
                          lr_bf[:])
                lowrank_T(xch, va1_sb, lrT_scr[:], lr_bf[:],
                          pre_pe=lambda w=a0_w: p.w("tensor", sW, w))
                bank = pb[i % 2]
                p.w("tensor", sAC)
                for hc in range(16):
                    mm(bank, xch(hc), wv_a1[:, hc * 512:(hc + 1) * 512],
                       hc == 0, False)
                mm(bank, lrT_scr[:], vb_a1[:], False, True, tok=True)
                p.w("scalar", sPE)
                p.w("scalar", sDV)
                so(lambda e, b=bank: e.activation(v_bf_a1[:], b, AF.Copy), tok=True)
                p.w("vector", sAC)
                vo(lambda e: e.tensor_tensor(q_sq[:], v_bf_a1[:], v_bf_a1[:],
                                             ALU.mult))
                vf()
                for g4 in range(4):
                    vo(lambda e, g4=g4: e.reduce_sum(
                        q_st[:, g4:g4 + 1], q_sq[:, g4 * 128:(g4 + 1) * 128],
                        axis=AX.X))
                vf()
                vo(lambda e: e.tensor_scalar_max(q_st[:, 0:4], q_st[:, 0:4], 1e-24))
                p.w("scalar", sDV)
                so(lambda e: e.activation(q_st[:, 4:8], q_st[:, 0:4], AF.Ln))
                sf()
                so(lambda e, tt=tt: e.activation(
                    rv_tm[:, tt * 4:(tt + 1) * 4], q_st[:, 4:8], AF.Exp,
                    scale=-0.5))
                p.w("tensor", sAC)
                p.w("tensor", sDV)
                for g4 in range(4):
                    transp(pbt[:, g4 * 128:(g4 + 1) * 128],
                           v_bf_a1[:, g4 * 128:(g4 + 1) * 128], tok=(g4 == 3))
                p.w("vector", sPE)
                vo(lambda e, tt=tt: e.tensor_copy(
                    v_ofm.rearrange("p (g c) -> p g c", c=2048)
                    [:, :, tt * 128:(tt + 1) * 128],
                    pbt.rearrange("p (a b) -> p a b", b=128)[:, 0:4, :]), tok=True)

            for qdt in range(4):
                slots = []
                for hh in range(2):
                    jj = th * 8 + qdt * 2 + hh
                    if jj >= 3:
                        p.w("gpsimd", sPE, wq_free[jj - 3])
                    dma_g(wq_ring[jj % 3], wq_i[qdt, hh])
                    slots.append((jj % 3, sW.n, jj))
                for i in range(8):
                    tt = th * 8 + i
                    bank = pb[2 + i % 2]
                    p.w("tensor", sAC)
                    for hh, (slot, wtok, jj) in enumerate(slots):
                        p.w("tensor", sW, wtok)
                        for hc8 in range(8):
                            hc = hh * 8 + hc8
                            mm(bank,
                               x_fmh[:, hc * 1024 + i * 128:hc * 1024 + (i + 1) * 128],
                               wq_ring[slot][:, hc8 * 512:(hc8 + 1) * 512],
                               hc == 0, False)
                    mm(bank, xlrT[0:16, i * 128:(i + 1) * 128],
                       qb_sb[0:16, qdt * 512:(qdt + 1) * 512], False, True, tok=True)
                    if i == 7:
                        wq_free[slots[0][2]] = sPE.n
                        wq_free[slots[1][2]] = sPE.n
                    p.w("scalar", sPE)
                    p.w("scalar", sDV)
                    so(lambda e, b=bank: e.activation(q_f32[:], b, AF.Copy),
                       tok=True)
                    p.w("vector", sAC)
                    vo(lambda e: e.tensor_tensor(q_sq[:], q_f32[:], q_f32[:],
                                                 ALU.mult))
                    vf()
                    for j in range(4):
                        vo(lambda e, j=j: e.reduce_sum(
                            q_st[:, j:j + 1], q_sq[:, j * 128:(j + 1) * 128],
                            axis=AX.X))
                    vf()
                    p.w("scalar", sDV)
                    so(lambda e: e.activation(q_st[:, 4:8], q_st[:, 0:4], AF.Ln,
                                              scale=1.0 / 128.0, bias=cns[:, 0:1]))
                    sf()
                    so(lambda e: e.activation(q_st[:, 8:12], q_st[:, 4:8], AF.Exp,
                                              scale=-0.5, bias=cns[:, 1:2]))
                    p.w("vector", sAC)
                    vo(lambda e, qdt=qdt: e.tensor_tensor(
                        q_st[:, 12:16], q_st[:, 8:12],
                        gain_bc[:, qdt * 4:(qdt + 1) * 4], ALU.mult))
                    vf()
                    for j in range(4):
                        vo(lambda e, j=j: e.tensor_scalar_mul(
                            q_f32[:, j * 128:(j + 1) * 128],
                            q_f32[:, j * 128:(j + 1) * 128],
                            q_st[:, 12 + j:13 + j]))
                    vf()
                    dst = q_tm[:, tt * 2048 + qdt * 512: tt * 2048 + (qdt + 1) * 512]
                    vo(lambda e, d=dst: e.tensor_copy(d, q_f32[:]))
                    rope_block(dst, q_f32, cosq[:, tt * 32:(tt + 1) * 32],
                               sinq[:, tt * 32:(tt + 1) * 32], q_sq)
        join()
        if "a1" in dbg:
            dma_st(ddbg("q_tm", [128, 32768], BF16), q_tm[:])
            dma_st(ddbg("v_ofm", [128, 8192], BF16), v_ofm[:])
            dma_st(ddbg("rv_tm", [128, 64], F32), rv_tm[:])

    if "a2" in phases:
        dma_s(cosk[:], cosk_i)
        dma_s(sink[:], sink_i)
        dma_g(wk_sb[:], wk_i)
        dma_g(wv_a1[:], wv_i)
        dma_g(ka_sb[:], ka_i)
        dma_g(kb_sb[:], kb_i)
        dma_g(va_sb[:], va_i)
        dma_g(vb_sb[:], vb_i)
        a2_w, a2_l = sW.n, sL.n
        stage_free = {}
        for tt in range(32):
            if tt >= 2:
                p.w("gpsimd", sPE, stage_free[tt - 2])
            dma_g(x_stage[tt % 2], xkv_i[tt * 128:(tt + 1) * 128, :])
            wtok = sW.n
            transpose_tile(
                x_stage[tt % 2],
                lambda g, n: x_fmkv.rearrange(
                    "p (a b) -> p a b", b=128)[:, g * 8:g * 8 + n, :],
                pre_pe=lambda w=wtok: p.w("tensor", sW, w))
            stage_free[tt] = sPE.n

            def xch(hc):
                return x_fmkv[:, hc * 128:(hc + 1) * 128]
            lowrank_T(xch, ka_sb, lrT_scr2[:], lr_bf2[:],
                      pre_pe=lambda w=a2_w: p.w("tensor", sW, w))
            bank = pb[tt % 2]
            p.w("tensor", sAC)
            for hc in range(16):
                mm(bank, xch(hc), wk_sb[:, hc * 512:(hc + 1) * 512], hc == 0, False)
            mm(bank, lrT_scr2[:], kb_sb[:], False, True, tok=True)
            p.w("scalar", sPE)
            p.w("scalar", sDV)
            so(lambda e, b=bank: e.activation(kvscr[:], b, AF.Copy), tok=True)
            p.w("vector", sAC)
            vo(lambda e: e.tensor_tensor(kvsq[:], kvscr[:], kvscr[:], ALU.mult))
            vf()
            for j in range(4):
                vo(lambda e, j=j: e.reduce_sum(
                    kvst[:, j:j + 1], kvsq[:, j * 128:(j + 1) * 128], axis=AX.X))
            vf()
            p.w("scalar", sDV)
            so(lambda e: e.activation(kvst[:, 4:8], kvst[:, 0:4], AF.Ln,
                                      scale=1.0 / 128.0, bias=cns[:, 0:1]))
            sf()
            so(lambda e: e.activation(kvst[:, 8:12], kvst[:, 4:8], AF.Exp,
                                      scale=-0.5))
            p.w("vector", sAC)
            p.w("vector", sL, a2_l)
            for j in range(4):
                vo(lambda e, j=j: e.tensor_scalar_mul(
                    kvscr[:, j * 128:(j + 1) * 128],
                    kvscr[:, j * 128:(j + 1) * 128], kvst[:, 8 + j:9 + j]))
            vf()
            vo(lambda e: e.tensor_copy(kv_bf[:], kvscr[:]))
            rope_block(kv_bf, kvscr, cosk[:, tt * 32:(tt + 1) * 32],
                       sink[:, tt * 32:(tt + 1) * 32], kvsq)
            p.w("tensor", sDV)
            for g4 in range(4):
                transp(pbt[:, g4 * 128:(g4 + 1) * 128],
                       kv_bf[:, g4 * 128:(g4 + 1) * 128], tok=(g4 == 3))
            p.w("vector", sPE)
            vo(lambda e, tt=tt: e.tensor_copy(
                k_fm.rearrange("p (g c) -> p g c", c=4096)
                [:, :, tt * 128:(tt + 1) * 128],
                pbt.rearrange("p (a b) -> p a b", b=128)[:, 0:4, :]), tok=True)
            lowrank_T(xch, va_sb, lrT_scr2[:], lr_bf2[:])
            bank = pb[2 + tt % 2]
            p.w("tensor", sAC)
            for hc in range(16):
                mm(bank, xch(hc), wv_a1[:, hc * 512:(hc + 1) * 512], hc == 0, False)
            mm(bank, lrT_scr2[:], vb_sb[:], False, True, tok=True)
            p.w("scalar", sPE)
            so(lambda e, b=bank, tt=tt: e.activation(vtm(tt), b, AF.Copy), tok=True)
        join()
        if "a2" in dbg:
            dma_st(ddbg("k_fm", [128, 16384], BF16), k_fm[:])
            dkv = ddbg("v_tmd", [128, 16384], BF16)
            dma_st(dkv[:, 0:8192], v_tm[:])
            dma_st(dkv[:, 8192:16384], v_tm2[:])

    if "b" in phases:
        dma_s(mqs_s[:], mqs_i)
        dma_s(msq_s[:], msq_i)
        b_l = sL.n
        iters = [(g, pp, hq) for g in range(4) for pp in range(8)
                 for hq in range(4)]
        NIT = len(iters)
        st_tok = {}
        wk_tok = {}
        bank_rel = {0: 0, 1: 0}
        sqexp_rel = {0: 0, 1: 0}
        attnv_tok = {}
        pbt_rd = 0
        pbs_rd = 0
        av_last = {}

        def emit_stats(i):
            nonlocal pbt_rd, pbs_rd
            g, pp, hq = iters[i]
            n_s = pp + 1
            sl = i % 2
            t = {}
            prev2 = wk_tok.get(i - 2)
            p.w("tensor", sDV, pbt_rd)
            for blk in range(2):
                tt = 2 * pp + blk
                transp(pbt[:, blk * 128:(blk + 1) * 128],
                       q_tm[:, tt * 2048 + (4 * g + hq) * 128:
                            tt * 2048 + (4 * g + hq) * 128 + 128],
                       tok=(blk == 1))
            tr_tok = sPE.n
            if prev2 is not None:
                p.w("vector", sPE, prev2["sq_last"])
            p.w("vector", sPE, tr_tok)
            qfm_tok = vo(lambda e, sl=sl: e.tensor_copy(qfm_c[sl], pbt[:, 0:256]))
            t["qfm"] = qfm_tok
            pbt_rd_local = qfm_tok
            if hq == 0:
                p.w("tensor", sDV, pbs_rd)
                transp(pbs[0:1, 0:128],
                       rv_tm[:, (2 * pp) * 4 + g:(2 * pp) * 4 + g + 1],
                       ident=idf32)
                transp(pbs[0:1, 128:256],
                       rv_tm[:, (2 * pp + 1) * 4 + g:(2 * pp + 1) * 4 + g + 1],
                       ident=idf32, tok=True)
                p.w("vector", sPE)
                rv_cp = vo(lambda e: e.tensor_copy(rvrow[0:1, :], pbs[0:1, 0:256]))
                p.w("vector", sDV, rv_cp)
                vo(lambda e, pl=pp % 2: e.tensor_tensor(
                    rv2row_d[0:1, pl * 256:pl * 256 + 256], rvrow[0:1, :],
                    rvrow[0:1, :], ALU.mult))
                pbs_rd = rv_cp
            prev_exp = st_tok.get(i - 1, {}).get("exp_last", 0)
            p.w("vector", sAC, prev_exp)
            p.w("vector", sL, b_l)
            for blk in range(2):
                par = blk
                mt_tok = 0
                for st in range(n_s):
                    bk = st % 2
                    p.w("tensor", sDV, bank_rel[bk])
                    p.w("tensor", sDV, qfm_tok)
                    mm(pb[bk], qfm_c[sl][:, blk * 128:(blk + 1) * 128],
                       k_fm[:, g * 4096 + st * 512:g * 4096 + (st + 1) * 512],
                       True, True, tok=True)
                    mmt = sPE.n
                    p.w("vector", sPE, mmt)
                    if st == n_s - 1:
                        vo(lambda e, bk=bk, st=st, par=par: e.tensor_tensor(
                            s_row[:, st * 512:(st + 1) * 512], pb[bk],
                            mqs_s[:, par * 512:(par + 1) * 512], ALU.add))
                        p.w("vector", sDV)
                        mt_tok = vo(lambda e, st=st: e.reduce_max(
                            mtile[:, st:st + 1],
                            s_row[:, st * 512:(st + 1) * 512], axis=AX.X))
                    else:
                        vo(lambda e, bk=bk, st=st: e.tensor_copy(
                            s_row[:, st * 512:(st + 1) * 512], pb[bk]))
                        mt_tok = vo(lambda e, bk=bk, st=st: e.reduce_max(
                            mtile[:, st:st + 1], pb[bk], axis=AX.X))
                    bank_rel[bk] = mt_tok
                p.w("vector", sDV, mt_tok)
                mr = vo(lambda e, n=n_s: e.reduce_max(mred[:], mtile[:, 0:n],
                                                      axis=AX.X))
                p.w("vector", sDV, mr)
                nb = vo(lambda e, blk=blk: e.tensor_scalar_mul(
                    negmbf[:, blk:blk + 1], mred[:], -1.0))
                p.w("vector", sDV, nb)
                n32 = vo(lambda e, blk=blk: e.tensor_copy(
                    negm32[:, blk:blk + 1], negmbf[:, blk:blk + 1]))
                p.w("scalar", sDV, n32)
                for st in range(n_s):
                    so(lambda e, st=st, blk=blk: e.activation(
                        junk[:], s_row[:, st * 512:(st + 1) * 512], AF.Exp,
                        bias=negm32[:, blk:blk + 1],
                        accum_out=ssum[:, st:st + 1]))
                exp_last = sAC.n
                p.w("vector", sAC, exp_last)
                t1r = vo(lambda e, n=n_s: e.reduce_sum(tmp1[:], ssum[:, 0:n],
                                                       axis=AX.X))
                p.w("vector", sDV, t1r)
                vo(lambda e, blk=blk: e.reciprocal(rsblk[:, blk:blk + 1],
                                                   tmp1[:]))
            t["exp_last"] = sAC.n
            t["rsblk"] = sDV.n
            t["pbt_rd_local"] = pbt_rd_local
            st_tok[i] = t

        def emit_stats_rows(i):
            nonlocal pbt_rd
            g, pp, hq = iters[i]
            sl = i % 2
            t = st_tok[i]
            prev2 = wk_tok.get(i - 2)
            p.w("tensor", sDV, t["rsblk"])
            p.w("tensor", sDV, t["pbt_rd_local"])
            transp(pbt[0:1, 256:384], negmbf[:, 0:1])
            transp(pbt[0:1, 384:512], negmbf[:, 1:2], tok=True)
            p.w("tensor", sDV, bank_rel[0])
            transp(pb[0][0:1, 0:128], rsblk[:, 0:1], ident=idf32)
            transp(pb[0][0:1, 128:256], rsblk[:, 1:2], ident=idf32, tok=True)
            rowtr = sPE.n
            p.w("vector", sPE, rowtr)
            if prev2 is not None:
                p.w("vector", sPE, prev2["sq_last"])
                p.w("vector", sPE, prev2["bc"])
            nm = vo(lambda e, sl=sl: e.tensor_copy(
                negmrow_d[0:1, sl * 256:sl * 256 + 256], pbt[0:1, 256:512]))
            rs = vo(lambda e, sl=sl: e.tensor_copy(
                rsrow_d[0:1, sl * 256:sl * 256 + 256], pb[0][0:1, 0:256]))
            bank_rel[0] = rs
            t["rows"] = rs
            pbt_rd = nm

        def emit_work(i):
            g, pp, hq = iters[i]
            h = 4 * g + hq
            n_s = pp + 1
            sl = i % 2
            nsc = 4 * n_s
            t = {}
            rows_tok = st_tok[i]["rows"]
            prev = wk_tok.get(i - 1)
            prev2 = wk_tok.get(i - 2)
            for sc in range(nsc):
                bk = 2 + sc % 2
                p.w("tensor", sAC, sqexp_rel[sc % 2])
                if prev is not None:
                    p.w("tensor", sDV, prev["proj"])
                p.w("tensor", sDV, rows_tok)
                mm(pb[bk][:, 0:256],
                   k_fm[:, g * 4096 + sc * 128:g * 4096 + (sc + 1) * 128],
                   qfm_c[sl], True, False)
                mm(pb[bk][:, 0:256], ones_bfr[:],
                   negmrow_d[0:1, sl * 256:sl * 256 + 256], False, True, tok=True)
                sq_tok = sPE.n
                boundary = sc >= nsc - 4
                mask_tok = None
                if boundary:
                    r4 = sc - (nsc - 4)
                    p.w("vector", sPE, sq_tok)
                    mask_tok = vo(lambda e, bk=bk, r4=r4: e.tensor_tensor(
                        pb[bk][:, 0:256], pb[bk][:, 0:256],
                        msq_s[:, r4 * 256:(r4 + 1) * 256], ALU.add))
                p.w("scalar", sPE, sq_tok)
                if mask_tok is not None:
                    p.w("scalar", sDV, mask_tok)
                old_av = attnv_tok.get((i, sc - 4))
                if old_av is not None:
                    p.w("scalar", sPE, old_av)
                elif i - 1 in av_last:
                    p.w("scalar", sPE, av_last[i - 1])
                so(lambda e, bk=bk, sc=sc: e.activation(
                    p_ring[sc % 4], pb[bk][:, 0:256], AF.Exp))
                exp_tok = sAC.n
                sqexp_rel[sc % 2] = exp_tok
                p.w("tensor", sAC, exp_tok)
                if sc == 0 and prev2 is not None:
                    p.w("tensor", sAC, prev2["ycopy"])
                mm(pb[4 + i % 2][:, 0:256], vtm(sc)[:, g * 128:(g + 1) * 128],
                   p_ring[sc % 4], sc == 0, sc == nsc - 1, tok=True)
                attnv_tok[(i, sc)] = sPE.n
            av_last[i] = sPE.n
            t["sq_last"] = sq_tok
            wk_tok[i] = t

        def emit_work_rej(i):
            nonlocal pbs_rd
            g, pp, hq = iters[i]
            h = 4 * g + hq
            n_s = pp + 1
            sl = i % 2
            nsc = 4 * n_s
            t = wk_tok[i]
            prev = wk_tok.get(i - 1)
            p.w("scalar", sPE, attnv_tok[(i, nsc - 1)])
            if prev is not None:
                p.w("scalar", sDV, prev["y1"])
            yc = so(lambda e, b2=pb[4 + i % 2][:, 0:256]: e.activation(
                y_ev[:], b2, AF.Copy))
            t["ycopy"] = yc
            p.w("vector", sAC, yc)
            if prev is not None:
                p.w("vector", sPE, prev["dot"])
            t1t = vo(lambda e, g=g, pp=pp: e.tensor_tensor(
                t1_b[:], y_ev[:],
                v_ofm[:, g * 2048 + pp * 256:g * 2048 + (pp + 1) * 256],
                ALU.mult))
            t["t1"] = t1t
            p.w("tensor", sDV, t1t)
            p.w("tensor", sDV, pbs_rd)
            mm(pbs[0:1, 256:512], ones_bfc[:], t1_b[:], True, True, tok=True)
            t["dot"] = sPE.n
            p.w("vector", sPE, t["dot"])
            if prev is not None:
                p.w("vector", sPE, prev["bc"])
            dc = vo(lambda e: e.tensor_copy(dotrow[0:1, :], pbs[0:1, 256:512]))
            pbs_rd = dc
            p.w("vector", sDV, dc)
            d2a = vo(lambda e, pl=pp % 2: e.tensor_tensor(
                dot2row[0:1, :], dotrow[0:1, :],
                rv2row_d[0:1, pl * 256:pl * 256 + 256], ALU.mult))
            p.w("vector", sDV, d2a)
            d2b = vo(lambda e, sl=sl: e.tensor_tensor(
                dot2row[0:1, :], dot2row[0:1, :],
                rsrow_d[0:1, sl * 256:sl * 256 + 256], ALU.mult))
            t["dot2"] = d2b
            p.w("tensor", sDV, d2b)
            mm(pb[2][:, 0:256], ones_f32r[:], dot2row[0:1, :], True, True)
            mm(pb[3][:, 0:256], ones_f32r[:],
               rsrow_d[0:1, sl * 256:sl * 256 + 256], True, True, tok=True)
            t["bc"] = sPE.n
            p.w("vector", sPE, t["bc"])
            y1t = vo(lambda e: e.tensor_tensor(y1_b[:], y_ev[:], pb[3][:, 0:256],
                                               ALU.mult))
            t["y1"] = y1t
            pj = vo(lambda e, g=g, pp=pp: e.tensor_tensor(
                proj_b[:],
                v_ofm[:, g * 2048 + pp * 256:g * 2048 + (pp + 1) * 256],
                pb[2][:, 0:256], ALU.mult))
            t["proj"] = pj
            p.w("vector", sDV, pj)
            vo(lambda e, h=h, pp=pp: e.tensor_tensor(
                q_tm.rearrange("p (t c) -> p t c", c=2048)
                [:, 2 * pp:2 * pp + 2, h * 128:(h + 1) * 128],
                y1_b[:].rearrange("p (a b) -> p a b", b=128),
                proj_b[:].rearrange("p (a b) -> p a b", b=128),
                ALU.subtract))

        emit_stats(0)
        emit_stats_rows(0)
        for i in range(NIT):
            if i + 1 < NIT:
                emit_stats(i + 1)
            emit_work(i)
            if i + 1 < NIT:
                emit_stats_rows(i + 1)
            emit_work_rej(i)
        join()
        if "b" in dbg:
            dma_st(ddbg("y_out", [128, 32768], BF16), q_tm[:])

    if "c" in phases:
        dma_g(pa_sb[:], pa_i)
        dma_g(pb_sb[:], pb_i)
        c_w = sW.n
        for tt in range(16):
            def ych(dc, tt=tt):
                return q_tm[:, tt * 2048 + dc * 128: tt * 2048 + (dc + 1) * 128]
            lowrank_T(ych, pa_sb, xpaT[0:16, tt * 128:(tt + 1) * 128],
                      lr_bf3[:],
                      pre_pe=lambda w=c_w: p.w("tensor", sW, w))
        wp_free = {}
        for dout in range(4):
            slots = []
            for hh in range(2):
                jj = dout * 2 + hh
                if jj >= 3:
                    p.w("gpsimd", sPE, wp_free[jj - 3])
                dma_g(wp_ring[jj % 3], wp_i[dout, hh])
                slots.append((jj % 3, sW.n, jj))
            for tt in range(16):
                if True:
                    p.w("sync", sDV)
                    dma_s(xsl[tt % 2],
                          xq_i[tt * 128:(tt + 1) * 128, dout * 512:(dout + 1) * 512])
                    xtok = sL.n
                bank = pb[tt % 2]
                p.w("tensor", sDV)
                for hh, (slot, wtok, jj) in enumerate(slots):
                    p.w("tensor", sW, wtok)
                    for dc8 in range(8):
                        dc = hh * 8 + dc8
                        mm(bank, q_tm[:, tt * 2048 + dc * 128:tt * 2048 + (dc + 1) * 128],
                           wp_ring[slot][:, dc8 * 512:(dc8 + 1) * 512],
                           dc == 0, False)
                mm(bank, xpaT[0:16, tt * 128:(tt + 1) * 128],
                   pb_sb[0:16, dout * 512:(dout + 1) * 512], False, True, tok=True)
                if tt == 15:
                    wp_free[slots[0][2]] = sPE.n
                    wp_free[slots[1][2]] = sPE.n
                p.w("vector", sPE)
                p.w("vector", sL, xtok)
                p.w("vector", sS)
                vo(lambda e, b2=bank, tt=tt: e.tensor_tensor(
                    zsl[tt % 2], b2, xsl[tt % 2], ALU.add))
                vf()
                vo(lambda e, tt=tt: e.tensor_tensor(zslq[:], zsl[tt % 2],
                                                    zsl[tt % 2], ALU.mult))
                vo(lambda e, tt=tt: e.reduce_sum(tmp2[:], zsl[tt % 2], axis=AX.X))
                vf()
                vo(lambda e: e.reduce_sum(tmp1[:], zslq[:], axis=AX.X))
                vf()
                if dout == 0:
                    vo(lambda e, tt=tt: e.tensor_copy(ln_sq[:, tt:tt + 1], tmp1[:]))
                    vo(lambda e, tt=tt: e.tensor_copy(ln_sum[:, tt:tt + 1], tmp2[:]))
                else:
                    vo(lambda e, tt=tt: e.tensor_tensor(
                        ln_sq[:, tt:tt + 1], ln_sq[:, tt:tt + 1], tmp1[:], ALU.add))
                    vo(lambda e, tt=tt: e.tensor_tensor(
                        ln_sum[:, tt:tt + 1], ln_sum[:, tt:tt + 1], tmp2[:],
                        ALU.add))
                p.w("sync", sDV)
                dma_st(zsp[tt, :, dout * 512:(dout + 1) * 512], zsl[tt % 2])
        vf()
        vo(lambda e: e.tensor_scalar_mul(ln_mu[:], ln_sum[:], 1.0 / 2048.0))
        vo(lambda e: e.tensor_scalar_mul(ln_r[:], ln_sq[:], 1.0 / 2048.0))
        vf()
        vo(lambda e: e.tensor_tensor(ln_sq[:], ln_mu[:], ln_mu[:], ALU.mult))
        vf()
        vo(lambda e: e.tensor_tensor(ln_r[:], ln_r[:], ln_sq[:], ALU.subtract))
        p.w("scalar", sDV)
        so(lambda e: e.activation(ln_r[:], ln_r[:], AF.Ln, bias=cns[:, 2:3]))
        sf()
        so(lambda e: e.activation(ln_r[:], ln_r[:], AF.Exp, scale=-0.5))
        zdone = sS.n
        for tt in range(16):
            p.w("sync", sS, zdone)
            p.w("sync", sDV)
            dma_s(zrel[:], zsp[tt])
            ztok = sL.n
            p.w("vector", sL, ztok)
            p.w("vector", sAC)
            p.w("vector", sS)
            vo(lambda e, tt=tt: e.tensor_scalar(
                x1bf[tt % 2], zrel[:], ln_mu[:, tt:tt + 1], ln_r[:, tt:tt + 1],
                ALU.subtract, ALU.mult), tok=True)
            x1tok = sDV.n
            if tt < 8:
                transpose_tile(
                    x1bf[tt % 2],
                    lambda g, n, tt=tt: x1_fm.rearrange(
                        "p (hc c) -> p hc c", c=1024)[:, g * 8:g * 8 + n,
                                                      tt * 128:(tt + 1) * 128],
                    pre_pe=lambda t=x1tok: p.w("tensor", sDV, t))
            else:
                p.w("sync", sDV, x1tok)
                dma_st(x1sp[tt - 8], x1bf[tt % 2])
        join()
        if "c" in dbg:
            dma_st(ddbg("x1fm", [128, 16384], BF16), x1_fm[:])
            dma_st(ddbg("lnmu", [128, 16], F32), ln_mu[:])
            dma_st(ddbg("lnr", [128, 16], F32), ln_r[:])

    if "moe" in phases:
        p.w("tensor", sDV)
        p.w("tensor", sL)
        mm(pbs[:, 0:16], ones_f32r[:], ebias_sb[:], True, True, tok=True)
        p.w("vector", sPE)
        vo(lambda e: e.tensor_copy(ebias_bc[:], pbs[:, 0:16]), tok=True)

        def emit_routing(half):
            def vr(fn, tok=True):
                vf()
                return vo(fn)
            for t8 in range(8):
                p.w("tensor", sDV)
                p.w("tensor", sAC)
                p.w("tensor", sW)
                for hc in range(16):
                    mm(pbs[:, 0:16],
                       x1_fm[:, hc * 1024 + t8 * 128:hc * 1024 + (t8 + 1) * 128],
                       gwt_sb[:, hc * 16:(hc + 1) * 16], hc == 0, hc == 15,
                       tok=(hc == 15))
                p.w("scalar", sPE)
                so(lambda e, t8=t8: e.activation(
                    s_all[:, t8 * 16:(t8 + 1) * 16], pbs[:, 0:16], AF.Sigmoid),
                   tok=True)
            p.w("vector", sAC)
            vr(lambda e: e.tensor_tensor(
                c_all.rearrange("p (t e) -> p t e", e=16),
                s_all.rearrange("p (t e) -> p t e", e=16),
                ebias_bc.rearrange("p (o e) -> p o e", o=1).broadcast_to((128, 8, 16)),
                ALU.add))
            cv = c_all.rearrange("p (t g e) -> p t g e", g=4, e=4)
            pairs = [(0, 1), (0, 2), (0, 3), (1, 2), (1, 3), (2, 3)]
            mv = mext.rearrange("p (s t g) -> p s t g", s=2, t=8, g=4)
            first = True
            for (i, j) in pairs:
                vr(lambda e, i=i, j=j: e.tensor_tensor(
                    mv[:, 1], cv[:, :, :, i], cv[:, :, :, j], ALU.add))
                if first:
                    vr(lambda e: e.tensor_copy(mv[:, 0], mv[:, 1]))
                    first = False
                else:
                    vr(lambda e: e.tensor_tensor(mv[:, 0], mv[:, 0], mv[:, 1],
                                                 ALU.max))
            vr(lambda e: e.tensor_copy(gsc.rearrange("p (t g) -> p t g", g=4),
                                       mv[:, 0]))
            gv = gsc.rearrange("p (t g) -> p t g", g=4)
            first = True
            for (i, j) in pairs:
                vr(lambda e, i=i, j=j: e.tensor_tensor(
                    mv[:, 1, :, 0:1], gv[:, :, i:i + 1], gv[:, :, j:j + 1],
                    ALU.min))
                if first:
                    vr(lambda e: e.tensor_copy(thg.rearrange("p (t o) -> p t o", o=1),
                                               mv[:, 1, :, 0:1]))
                    first = False
                else:
                    vr(lambda e: e.tensor_tensor(
                        thg.rearrange("p (t o) -> p t o", o=1),
                        thg.rearrange("p (t o) -> p t o", o=1), mv[:, 1, :, 0:1], ALU.max))
            vr(lambda e: e.tensor_tensor(
                gmask.rearrange("p (t g) -> p t g", g=4), gv,
                thg.rearrange("p (t o) -> p t o", o=1).broadcast_to((128, 8, 4)), ALU.is_ge))
            vr(lambda e: e.tensor_scalar(
                r_scr.rearrange("p (t g e) -> p t g e", g=4, e=4),
                gmask.rearrange("p (t g o) -> p t g o", g=4, o=1).broadcast_to((128, 8, 4, 4)),
                1e30, 1e30, ALU.mult, ALU.subtract))
            vr(lambda e: e.tensor_tensor(cm_all[:], c_all[:], r_scr[:], ALU.add))
            vr(lambda e: e.tensor_copy(r_scr2[:], cm_all[:]))
            for rnd in range(4):
                ev = r_scr2.rearrange("p (t e) -> p t e", e=16)
                vr(lambda e: e.tensor_tensor(
                    mext.rearrange("p (t e) -> p t e", e=8)[:, 0:8],
                    ev[:, :, 0:8], ev[:, :, 8:16], ALU.max))
                m8 = mext.rearrange("p (t e) -> p t e", e=8)
                vr(lambda e: e.tensor_tensor(m8[:, :, 0:4], m8[:, :, 0:4],
                                             m8[:, :, 4:8], ALU.max))
                vr(lambda e: e.tensor_tensor(m8[:, :, 0:2], m8[:, :, 0:2],
                                             m8[:, :, 2:4], ALU.max))
                vr(lambda e: e.tensor_tensor(m8[:, :, 0:1], m8[:, :, 0:1],
                                             m8[:, :, 1:2], ALU.max))
                if rnd < 3:
                    vr(lambda e: e.tensor_tensor(
                        r_scr.rearrange("p (t e) -> p t e", e=16), ev,
                        m8[:, :, 0:1].broadcast_to((128, 8, 16)), ALU.is_ge))
                    vr(lambda e: e.tensor_scalar(
                        r_scr[:], r_scr[:], -2e30, None, ALU.mult))
                    vr(lambda e: e.tensor_tensor(r_scr2[:], r_scr2[:], r_scr[:],
                                                 ALU.add))
            vr(lambda e: e.tensor_tensor(
                w_all.rearrange("p (t e) -> p t e", e=16),
                cm_all.rearrange("p (t e) -> p t e", e=16),
                mext.rearrange("p (t e) -> p t e", e=8)[:, :, 0:1]
                .broadcast_to((128, 8, 16)), ALU.is_ge))
            vr(lambda e: e.tensor_tensor(w_all[:], w_all[:], s_all[:], ALU.mult))
            wv_ = w_all.rearrange("p (t e) -> p t e", e=16)
            vr(lambda e: e.tensor_tensor(
                mext.rearrange("p (t e) -> p t e", e=8)[:, 0:8],
                wv_[:, :, 0:8], wv_[:, :, 8:16], ALU.add))
            m8 = mext.rearrange("p (t e) -> p t e", e=8)
            vr(lambda e: e.tensor_tensor(m8[:, :, 0:4], m8[:, :, 0:4],
                                         m8[:, :, 4:8], ALU.add))
            vr(lambda e: e.tensor_tensor(m8[:, :, 0:2], m8[:, :, 0:2],
                                         m8[:, :, 2:4], ALU.add))
            vr(lambda e: e.tensor_tensor(wsum.rearrange("p (t o) -> p t o", o=1),
                                         m8[:, :, 0:1], m8[:, :, 1:2], ALU.add))
            vr(lambda e: e.tensor_scalar_add(wsum[:], wsum[:], 1e-20))
            vr(lambda e: e.reciprocal(wsum[:], wsum[:]))
            vr(lambda e: e.tensor_scalar_mul(wsum[:], wsum[:], 2.5))
            for t8 in range(8):
                vr(lambda e, t8=t8: e.tensor_scalar_mul(
                    w_all[:, t8 * 16:(t8 + 1) * 16], w_all[:, t8 * 16:(t8 + 1) * 16],
                    wsum[:, t8:t8 + 1]), tok=(t8 == 7))

        gu_free = {}
        dw_free = {}
        gu_cnt = 0
        dw_cnt = 0
        for half in range(2):
            if moe_stop == "routing" and half == 1:
                break
            if half == 1:
                for i in range(8):
                    p.w("sync", sPE)
                    dma_s(x1ld[i % 2], x1sp[i])
                    ztok = sL.n
                    transpose_tile(
                        x1ld[i % 2],
                        lambda g, n, i=i: x1_fm.rearrange(
                            "p (hc c) -> p hc c", c=1024)[:, g * 8:g * 8 + n,
                                                          i * 128:(i + 1) * 128],
                        pre_pe=lambda w=ztok: p.w("tensor", sL, w))
            emit_routing(half)
            if moe_stop == "routing":
                join()
                if half == 0:
                    dma_st(ddbg("m_wall", [128, 128], F32), w_all[:])
                    dma_st(ddbg("m_sall", [128, 128], F32), s_all[:])
                return_early = True
                break
            for e_i in range(18):
                if moe_stop is not None and isinstance(moe_stop, tuple) \
                        and moe_stop[0] == "expert" and (half, e_i) > moe_stop[1:]:
                    break
                routedexp = e_i < 16
                for ft in range(8):
                    gu = gu_cnt; gu_cnt += 1
                    slot = gu % 3
                    if gu >= 3:
                        p.w("gpsimd", sPE, gu_free[gu - 3])
                    if routedexp:
                        dma_g(g_ring[slot], gate_i[e_i, ft])
                        dma_g(u_ring[slot], up_i[e_i, ft])
                    else:
                        se = e_i - 16
                        dma_g(g_ring[slot], shg_i[se * 8 + ft])
                        dma_g(u_ring[slot], shu_i[se * 8 + ft])
                    wtok = sW.n
                    for t2 in range(2):
                        gb = pb[t2]
                        ub = pb[2 + t2]
                        p.w("tensor", sAC)
                        p.w("tensor", sDV)
                        p.w("tensor", sW, wtok)
                        for hc in range(16):
                            mm(gb, g_ring[slot][:, hc * 128:(hc + 1) * 128],
                               x1_fm[:, hc * 1024 + t2 * 512:hc * 1024 + (t2 + 1) * 512],
                               hc == 0, hc == 15)
                        for hc in range(16):
                            mm(ub, u_ring[slot][:, hc * 128:(hc + 1) * 128],
                               x1_fm[:, hc * 1024 + t2 * 512:hc * 1024 + (t2 + 1) * 512],
                               hc == 0, hc == 15, tok=(hc == 15))
                        if t2 == 1:
                            gu_free[gu] = sPE.n
                        p.w("scalar", sPE)
                        p.w("scalar", sDV)
                        so(lambda e, gb=gb, t2=t2: e.activation(
                            silu_s[t2], gb, AF.Silu), tok=True)
                        p.w("vector", sAC)
                        vo(lambda e, ub=ub, ft=ft, t2=t2: e.tensor_tensor(
                            act_sb[:, ft * 1024 + t2 * 512:ft * 1024 + (t2 + 1) * 512],
                            silu_s[t2], ub, ALU.mult), tok=True)
                act_tok = sDV.n
                for ht in range(4):
                    dwi = dw_cnt; dw_cnt += 1
                    slot = dwi % 2
                    if dwi >= 2:
                        p.w("gpsimd", sPE, dw_free[dwi - 2])
                    if routedexp:
                        dma_g(dw_ring[slot], down_i[e_i, ht])
                    else:
                        dma_g(dw_ring[slot], shd_i[e_i - 16, ht])
                    wtok = sW.n
                    for t8 in range(8):
                        bank = pb[4 + t8 % 2]
                        p.w("tensor", sDV)
                        p.w("tensor", sDV, act_tok)
                        p.w("tensor", sW, wtok)
                        for fc in range(8):
                            mm(bank,
                               act_sb[:, fc * 1024 + t8 * 128:fc * 1024 + (t8 + 1) * 128],
                               dw_ring[slot][:, fc * 512:(fc + 1) * 512],
                               fc == 0, fc == 7, tok=(fc == 7))
                        if t8 == 7:
                            dw_free[dwi] = sPE.n
                        dst = routed[:, t8 * 2048 + ht * 512:t8 * 2048 + (ht + 1) * 512]
                        p.w("vector", sPE)
                        if e_i == 0:
                            p.w("vector", sS)
                            vo(lambda e, b2=bank, d=dst, t8=t8: e.tensor_scalar(
                                d, b2, w_all[:, t8 * 16 + 0:t8 * 16 + 1], None,
                                ALU.mult), tok=True)
                        elif routedexp:
                            vo(lambda e, b2=bank, d=dst, t8=t8, e_i=e_i:
                               e.scalar_tensor_tensor(
                                   d, b2, w_all[:, t8 * 16 + e_i:t8 * 16 + e_i + 1],
                                   d, ALU.mult, ALU.add), tok=True)
                        else:
                            vo(lambda e, b2=bank, d=dst: e.tensor_tensor(
                                d, b2, d, ALU.add), tok=True)
            if moe_stop is not None and isinstance(moe_stop, tuple) \
                    and moe_stop[0] == "expert":
                join()
                dma_st(ddbg("m_routed", [128, 16384], F32), routed[:])
                break
            for t8 in range(8):
                tt = half * 8 + t8
                p.w("sync", sDV)
                dma_s(zrel2[t8 % 2], zsp[tt])
                ztok = sL.n
                p.w("vector", sL, ztok)
                vo(lambda e, t8=t8, tt=tt: e.tensor_scalar(
                    zrel2[t8 % 2], zrel2[t8 % 2], ln_mu[:, tt:tt + 1],
                    ln_r[:, tt:tt + 1], ALU.subtract, ALU.mult))
                dst = routed[:, t8 * 2048:(t8 + 1) * 2048]
                p.w("vector", sS)
                vf()
                vo(lambda e, d=dst, t8=t8: e.tensor_tensor(d, d, zrel2[t8 % 2],
                                                           ALU.add))
                vf()
                vo(lambda e, d=dst: e.tensor_tensor(ln2scr[:], d, d, ALU.mult))
                vo(lambda e, d=dst: e.reduce_sum(tmp2[:], d, axis=AX.X))
                vf()
                vo(lambda e: e.reduce_sum(tmp1[:], ln2scr[:], axis=AX.X))
                vf()
                vo(lambda e: e.tensor_scalar_mul(ln2st[:, 2:3], tmp2[:],
                                                 1.0 / 2048.0))
                vo(lambda e: e.tensor_scalar_mul(ln2st[:, 3:4], tmp1[:],
                                                 1.0 / 2048.0))
                vf()
                vo(lambda e: e.tensor_tensor(ln2st[:, 0:1], ln2st[:, 2:3],
                                             ln2st[:, 2:3], ALU.mult))
                vf()
                vo(lambda e: e.tensor_tensor(ln2st[:, 0:1], ln2st[:, 3:4],
                                             ln2st[:, 0:1], ALU.subtract))
                p.w("scalar", sDV)
                so(lambda e: e.activation(ln2st[:, 1:2], ln2st[:, 0:1], AF.Ln,
                                          bias=cns[:, 2:3]))
                sf()
                so(lambda e: e.activation(ln2st[:, 1:2], ln2st[:, 1:2], AF.Exp,
                                          scale=-0.5))
                p.w("vector", sAC)
                vo(lambda e, d=dst: e.tensor_scalar(
                    d, d, ln2st[:, 2:3], ln2st[:, 1:2], ALU.subtract, ALU.mult))
                p.w("sync", sDV)
                dma_st(out_d[tt * 128:(tt + 1) * 128, :], dst)

    join()
    with nc.Block() as block:
        p.emit(block)
    ctx.close()
    return nc, dbg_outs


def _in_maps(inputs):
    sh = _prep_shared(inputs)
    shared_keys = ("wq", "wk", "wv", "wproj", "qa", "qb", "ka", "kb", "va",
                   "vb", "pa", "pb", "gain", "gwt", "ebias", "gate_t", "up_t",
                   "down_t", "shg_t", "shu_t", "shd_t", "cosk", "sink",
                   "idbf", "idf32")
    maps = []
    for c in range(8):
        m = {k: sh[k] for k in shared_keys}
        m.update(_prep_core(inputs, sh, c))
        maps.append(m)
    return maps


def kernel(**inputs):
    maps = _in_maps(inputs)
    nc, _ = build_program()
    res = run_bass_kernel_spmd(nc, maps, core_ids=list(range(8)))
    out = np.empty((B, T, D), np.float32)
    for c in range(8):
        b, hc = c // 2, c % 2
        oc = np.asarray(res.results[c]["out"], np.float32)
        out[b].reshape(32, 128, D)[hc::2] = oc.reshape(16, 128, D)
    return out
